# version 04 (final)
# baseline (speedup 1.0000x reference)
import math
from contextlib import ExitStack

import numpy as np
import ml_dtypes

import concourse.bass as bass
import concourse.mybir as mybir
from concourse.bass_utils import run_bass_kernel_spmd

F32 = mybir.dt.float32
BF16 = mybir.dt.bfloat16
AF = mybir.ActivationFunctionType
AX = mybir.AxisListType
OP = mybir.AluOpType

T = 2048
D = 1024
NDC = D // 128
NTC = T // 512
NQT = T // 128
EPS = 1e-5
NEG = -1e30

SCRATCH_BYTES = 104 * 1024
LN_OFF = SCRATCH_BYTES - 12 * 1024

AOF = dict(o=0, wsA=32 * 1024, wsB=40 * 1024, q=48 * 1024, k=52 * 1024,
           v=56 * 1024, p=60 * 1024, pt=68 * 1024, bw=84 * 1024)



def _blk(w):
    K, F = w.shape
    assert K % 128 == 0 and F % 512 == 0, (K, F)
    return np.ascontiguousarray(
        w.reshape(K // 128, 128, F // 512, 512).transpose(0, 2, 1, 3))


def _rope_perm():
    p = np.arange(3 * D)
    for part in range(2):
        for h in range(16):
            base = part * D + h * 64
            idx = np.concatenate([np.arange(0, 64, 2), np.arange(1, 64, 2)])
            p[base:base + 64] = base + idx
    return p


def _rope_tables():
    inv = 1.0 / (10000.0 ** (np.arange(0, 64, 2, dtype=np.float64) / 64))
    t = np.arange(T, dtype=np.float64)
    fr = np.outer(inv, t)
    cos4 = np.tile(np.cos(fr), (4, 1)).astype(np.float32)
    sgn = np.ones((128, 1))
    sgn[0:32] = -1.0
    sgn[64:96] = -1.0
    sin4 = (np.tile(np.sin(fr), (4, 1)) * sgn).astype(np.float32)
    return cos4, sin4


def _swap_mat():
    s = np.zeros((128, 128), dtype=np.float32)
    for h in range(2):
        b0 = h * 64
        for i in range(32):
            s[b0 + i, b0 + 32 + i] = 1.0
            s[b0 + 32 + i, b0 + i] = 1.0
    return s


def _rel_onehotT():
    NB, MAXD = 32, 128
    rel = np.arange(-2047, 2048)
    nb = NB // 2
    buckets = np.where(rel > 0, nb, 0)
    arel = np.abs(rel)
    mx = nb // 2
    small = arel < mx
    with np.errstate(divide="ignore"):
        tmp = np.log(np.maximum(arel, 1).astype(np.float32) / mx)
    large = tmp / math.log(MAXD / mx) * (nb - mx)
    large = mx + large.astype(np.int32)
    large = np.minimum(large, nb - 1)
    bk = buckets + np.where(small, arel, large)
    oh = np.zeros((32, 4096), dtype=np.float32)
    oh[bk, np.arange(4095)] = 1.0
    return oh


def make_weight_map(inp):
    g = {}
    perm = _rope_perm()
    g["l0_wqkv"] = _blk(np.asarray(inp["L0_attn_w_qkv"], np.float32)[:, perm])
    g["l0_wout"] = _blk(np.asarray(inp["L0_attn_w_out"], np.float32))
    g["l0_wr"] = np.ascontiguousarray(
        np.asarray(inp["L0_ffn_w_router"], np.float32).reshape(NDC, 128).T)
    g["l0_w1"] = _blk(np.asarray(inp["L0_ffn_W1"], np.float32))
    g["l0_b1"] = np.asarray(inp["L0_ffn_b1"], np.float32).reshape(1, -1).copy()
    g["l0_w2"] = _blk(np.asarray(inp["L0_ffn_W2"], np.float32))
    g["l0_b2"] = np.asarray(inp["L0_ffn_b2"], np.float32).reshape(1, -1).copy()
    g["l1_wqkv"] = _blk(np.asarray(inp["L1_attn_w_qkv"], np.float32))
    g["l1_bqkv"] = np.asarray(inp["L1_attn_b_qkv"], np.float32).reshape(1, -1).copy()
    g["l1_wd"] = _blk(np.asarray(inp["L1_attn_w_dense"], np.float32))
    g["l1_bd"] = np.asarray(inp["L1_attn_b_dense"], np.float32).reshape(1, -1).copy()
    g["l1_tbl"] = np.asarray(inp["L1_attn_rel_attn_table"], np.float32).copy()
    g["l1_rwT"] = np.ascontiguousarray(
        np.asarray(inp["L1_ffn_router_w"], np.float32).T.reshape(NDC, 128, 8)
        .transpose(1, 0, 2).reshape(128, NDC * 8))
    g["l1_rb"] = np.asarray(inp["L1_ffn_router_b"], np.float32).reshape(1, 8).copy()
    wup = np.asarray(inp["L1_ffn_w_up"], np.float32)
    wdn = np.asarray(inp["L1_ffn_w_down"], np.float32)
    sup = np.asarray(inp["L1_ffn_shared_w_up"], np.float32)
    sdn = np.asarray(inp["L1_ffn_shared_w_down"], np.float32)
    ups = [sup[0].T] + [wup[e].T for e in range(8)]
    dns = [sdn[0].T] + [wdn[e].T for e in range(8)]
    g["l1_wup"] = np.ascontiguousarray(np.stack([_blk(u) for u in ups]))
    g["l1_wdn"] = np.ascontiguousarray(np.stack([_blk(d) for d in dns]))
    iden = np.eye(128, dtype=np.float32)
    irev = np.zeros((128, 128), dtype=np.float32)
    irev[np.arange(128), 127 - np.arange(128)] = 1.0
    cm = np.zeros((128, 128), dtype=np.float32)
    cm[np.triu_indices(128, 1)] = NEG
    g["c_iden"] = iden
    g["c_irev"] = irev
    g["c_cmask"] = cm
    g["c_negf"] = np.full((128, 128), NEG, dtype=np.float32)
    g["c_swap"] = _swap_mat()
    cos4, sin4 = _rope_tables()
    g["c_cos2"] = cos4
    g["c_sin2"] = sin4
    g["c_oht"] = _rel_onehotT()
    return g



class Ctr:
    def __init__(self):
        self.v = {}

    def bump(self, k, n=1):
        self.v[k] = self.v.get(k, 0) + n
        return self.v[k]

    def get(self, k):
        return self.v.get(k, 0)


class B:
    has_l0a = False

    def __init__(self, nc, es):
        self.nc = nc
        self.es = es
        self.sems = {}
        self.t = {}
        self.eng = None
        self.E = None
        self.c = None
        self.dbg = []

    def sem(self, name):
        if name not in self.sems:
            self.sems[name] = self.es.enter_context(self.nc.semaphore(name))
        return self.sems[name]

    def on(self, *engs):
        return self.eng in engs

    def carve(self, byte_off, part, n, dt):
        assert byte_off % 4 == 0
        assert byte_off + n * (2 if dt == BF16 else 4) <= SCRATCH_BYTES, (byte_off, n)
        sc = self.t["scratch"]
        if dt == BF16:
            return sc[0:part, byte_off // 2: byte_off // 2 + n]
        return sc[0:part, byte_off // 2: byte_off // 2 + 2 * n].bitcast(F32)

    def fence(self, name, engs=("tensor", "vector", "scalar", "gpsimd")):
        s = self.sem("f_" + name)
        if self.eng in engs:
            if self.eng == "vector":
                self.E.tensor_copy(self.t["fdv"][:, :], self.t["fdv"][:, :]).then_inc(s, 1)
            elif self.eng == "scalar":
                self.E.activation(out=self.t["fds"][:, :], in_=self.t["fds"][:, :],
                                  func=AF.Identity).then_inc(s, 1)
            elif self.eng == "gpsimd":
                self.E.memset(self.t["fdg"][:, :], 0.0).then_inc(s, 1)
            else:
                self.E.nop().then_inc(s, 1)
            self.E.wait_ge(s, len(engs))

    def dump(self, name, ap, wait_sem, wait_val, block=("scalar", "vector", "gpsimd")):
        if name not in self.dbg:
            return
        s = self.sem("dbg")
        self.c.bump("dbg", 16)
        if self.on("sync"):
            self.E.wait_ge(wait_sem, wait_val)
            if ap.dtype == F32:
                ap = ap.bitcast(BF16)
            self.E.dma_start(out=self.t["dbg_" + name].ap()[0:ap.shape[0], 0:ap.shape[1]],
                             in_=ap).then_inc(s, 16)
        if self.eng in block:
            self.E.wait_ge(s, self.c.get("dbg"))


def x_ap(b, c, sl=None):
    if sl is None:
        return b.t["x"][:, c * T:(c + 1) * T]
    return b.t["x"][:, c * T + sl.start: c * T + sl.stop]


def xln_ap(b, c, sl=None):
    if sl is None:
        return b.t["xln"][:, c * T:(c + 1) * T]
    return b.t["xln"][:, c * T + sl.start: c * T + sl.stop]


def ln_aps(b):
    o = LN_OFF
    cast = [b.carve(o, 128, 512, BF16), b.carve(o + 1024, 128, 512, BF16)]
    sq = [b.carve(o + 2048, 128, 512, BF16), b.carve(o + 3072, 128, 512, BF16)]
    lntmp = b.carve(o + 4096, 128, 512, F32)
    rows = b.carve(o + 6144, 1, 1536, F32)
    return cast, sq, lntmp, rows


R_RSTD, R_NMR, R_M = slice(0, 512), slice(512, 1024), slice(1024, 1536)



def emit_load_x(b):
    x = b.t["x_in"].ap()
    s = b.sem("x_ld")
    for c in range(NDC):
        b.c.bump("x_ld", 16)
        if b.on("sync"):
            b.E.dma_start(out=x_ap(b, c), in_=x[c * 128:(c + 1) * 128, :]).then_inc(s, 16)


def emit_consts(b, l0a):
    s = b.sem("cst")
    moves = [("iden_bf", "c_iden"), ("irev_bf", "c_irev")]
    if l0a:
        moves += [("cmask_bf", "c_cmask"), ("negf_bf", "c_negf"),
                  ("swap_bf", "c_swap")]
    for dst, src in moves:
        b.c.bump("cst", 16)
        if b.on("gpsimd"):
            b.E.dma_start(out=b.t[dst][:, :], in_=b.t[src].ap()[:, :]).then_inc(s, 16)
    sm = b.sem("cstm")
    b.c.bump("cstm", 1)
    if b.on("gpsimd"):
        b.E.memset(b.t["ones_b"][:, :], 1.0)
        b.E.memset(b.t["onescol_f"][:, :], 1.0)
        b.E.memset(b.t["onesrow_f"][:, :], 1.0)
        b.E.memset(b.t["nonesrow_f"][:, :], -1.0)
        b.E.memset(b.t["onesrow_b"][:, :], 1.0)
        b.E.memset(b.t["epsc"][:, :], EPS).then_inc(sm, 1)
    if b.on("tensor", "vector", "scalar"):
        b.E.wait_ge(s, b.c.get("cst"))
        b.E.wait_ge(sm, b.c.get("cstm"))


def emit_bias_cols(b, l0f, l1a):
    jobs = []
    if l1a:
        jobs += [("l1_bqkv", 0, 8, "bq8_col", 0.125), ("l1_bqkv", 8, 8, "bk_col", None),
                 ("l1_bqkv", 16, 8, "bv_col", None), ("l1_bd", 0, 8, "bd_col", None)]
    if l0f:
        jobs += [("l0_b1", 0, 32, "b1_col", None), ("l0_b2", 0, 8, "b2_col", None)]
    if not jobs:
        return
    s_ld = b.sem("bcl")
    s_t = b.sem("bct")
    s_v = b.sem("bcv")
    row = b.carve(0, 1, 4096, F32)
    iden16 = b.t["iden16"]
    b.c.bump("bcl", 16)
    if b.on("sync"):
        b.E.dma_start(out=iden16[0:16, 0:16],
                      in_=b.t["c_iden"].ap()[0:16, 0:16]).then_inc(s_ld, 16)
    for src, off, n, dst, scale in jobs:
        b.c.bump("bcl", 16)
        if b.on("sync"):
            if b.c.get("bcv"):
                b.E.wait_ge(s_v, b.c.get("bcv"))
            b.E.dma_start(out=row[0:1, 0:n * 128],
                          in_=b.t[src].ap()[0:1, off * 128:(off + n) * 128]).then_inc(s_ld, 16)
        nld = b.c.get("bcl")
        for i in range(n):
            nt = b.c.bump("bct")
            if b.on("tensor"):
                if i == 0:
                    b.E.wait_ge(s_ld, nld)
                if nt > 2:
                    b.E.wait_ge(s_v, nt - 2)
                b.E.transpose(b.t["psA"][:, (i % 2):(i % 2) + 1],
                              row[0:1, i * 128:(i + 1) * 128],
                              iden16[0:1, 0:1]).then_inc(s_t, 1)
            b.c.bump("bcv")
            if b.on("vector"):
                b.E.wait_ge(s_t, nt)
                if scale is None:
                    b.E.tensor_copy(b.t[dst][:, i:i + 1],
                                    b.t["psA"][:, (i % 2):(i % 2) + 1])
                else:
                    b.E.tensor_scalar_mul(b.t[dst][:, i:i + 1],
                                          b.t["psA"][:, (i % 2):(i % 2) + 1], scale)
                b.E.nop().then_inc(s_v, 1)
    b.fence("bcols", ("tensor", "vector", "scalar", "sync"))


def emit_ln_tch(b, tag, tch, src_fn, mode, guard=None):
    grp = b.t["psA"] if tch % 2 == 0 else b.t["psB"]
    sl = slice(tch * 512, (tch + 1) * 512)
    cast, sq, lntmp, rows = ln_aps(b)
    s_c = b.sem(tag + "c")
    s_q = b.sem(tag + "q")
    s_m = b.sem(tag + "m")
    s_r = b.sem(tag + "r")
    s_bc = b.sem(tag + "bc")
    s_n = b.sem(tag + "n")
    prev_n = b.c.get(tag + "n")

    for c in range(NDC):
        slot = c % 2
        src, needs_cast = src_fn(c)
        if needs_cast:
            b.c.bump(tag + "c")
            if b.on("vector"):
                if b.c.get(tag + "m") >= 2:
                    b.E.wait_ge(s_m, b.c.get(tag + "m") - 1)
                b.E.tensor_copy(cast[slot], src).then_inc(s_c, 1)
            src_bf = cast[slot]
        else:
            src_bf = src
        nq = b.c.bump(tag + "q")
        if b.on("scalar"):
            if needs_cast:
                b.E.wait_ge(s_c, b.c.get(tag + "c"))
            elif guard is not None and c == 0:
                b.E.wait_ge(guard[0], guard[1])
            if b.c.get(tag + "m") >= 2:
                b.E.wait_ge(s_m, b.c.get(tag + "m") - 1)
            b.E.activation(out=sq[slot], in_=src_bf, func=AF.Square).then_inc(s_q, 1)
        b.c.bump(tag + "m")
        if b.on("tensor"):
            if c == 0:
                if guard is not None:
                    b.E.wait_ge(guard[0], guard[1])
                if tch >= 2:
                    b.E.wait_ge(s_n, prev_n - (NDC if mode == "pre" else NDC))
            b.E.wait_ge(s_q, nq)
            b.E.matmul(grp[0:1, 1024:1536], b.t["ones_b"][:, 0:1], src_bf,
                       start=(c == 0), stop=(c == NDC - 1))
            b.E.matmul(grp[0:1, 1536:2048], b.t["ones_b"][:, 0:1], sq[slot],
                       start=(c == 0), stop=(c == NDC - 1)).then_inc(s_m, 1)
    nr0a = b.c.bump(tag + "r")
    nr0b = b.c.bump(tag + "r")
    nr0 = b.c.bump(tag + "r")
    if b.on("vector"):
        b.E.wait_ge(s_m, b.c.get(tag + "m"))
        b.E.tensor_scalar_mul(rows[0:1, R_M], grp[0:1, 1024:1536],
                              1.0 / D).then_inc(s_r, 1)
        b.E.wait_ge(s_r, nr0a)
        b.E.tensor_tensor(rows[0:1, R_NMR], rows[0:1, R_M], rows[0:1, R_M],
                          OP.mult).then_inc(s_r, 1)
        b.E.wait_ge(s_r, nr0b)
        b.E.scalar_tensor_tensor(rows[0:1, R_RSTD], grp[0:1, 1536:2048], 1.0 / D,
                                 rows[0:1, R_NMR], OP.mult,
                                 OP.subtract).then_inc(s_r, 1)
    nr1a = b.c.bump(tag + "r")
    nr1 = b.c.bump(tag + "r")
    if b.on("scalar"):
        b.E.wait_ge(s_r, nr0)
        b.E.activation(out=rows[0:1, R_RSTD], in_=rows[0:1, R_RSTD], func=AF.Ln,
                       bias=EPS).then_inc(s_r, 1)
        b.E.wait_ge(s_r, nr1a)
        b.E.activation(out=rows[0:1, R_RSTD], in_=rows[0:1, R_RSTD], func=AF.Exp,
                       scale=-0.5).then_inc(s_r, 1)
    nr2 = b.c.bump(tag + "r")
    if b.on("vector"):
        b.E.wait_ge(s_r, nr1)
        b.E.tensor_tensor(rows[0:1, R_NMR], rows[0:1, R_M],
                          rows[0:1, R_RSTD], OP.mult).then_inc(s_r, 1)
    nbc = b.c.bump(tag + "bc")
    if b.on("tensor"):
        b.E.wait_ge(s_r, nr2)
        b.E.matmul(grp[:, 1024:1536], b.t["onesrow_f"][0:1, :], rows[0:1, R_RSTD],
                   start=True, stop=True)
        b.E.matmul(grp[:, 1536:2048], b.t["nonesrow_f"][0:1, :], rows[0:1, R_NMR],
                   start=True, stop=True).then_inc(s_bc, 1)
    for c in range(NDC):
        src, _ = src_fn(c)
        b.c.bump(tag + "n")
        if b.on("vector"):
            if c == 0:
                b.E.wait_ge(s_bc, nbc)
            if mode == "pre":
                b.E.tensor_tensor(lntmp, x_ap(b, c, sl), grp[:, 1024:1536], OP.mult)
                b.E.tensor_tensor(xln_ap(b, c, sl), lntmp,
                                  grp[:, 1536:2048], OP.add).then_inc(s_n, 1)
            else:
                b.E.tensor_tensor(lntmp, src, grp[:, 1024:1536], OP.mult)
                b.E.tensor_tensor(lntmp, lntmp, grp[:, 1536:2048], OP.add)
                b.E.tensor_tensor(x_ap(b, c, sl), x_ap(b, c, sl), lntmp,
                                  OP.add).then_inc(s_n, 1)
    return rows


def emit_pre_ln(b, tag, extra_cb=None):
    if b.on("vector", "scalar"):
        b.E.wait_ge(b.sems["x_ld"], b.c.get("x_ld"))
    for tch in range(NTC):
        rows = emit_ln_tch(
            b, tag, tch,
            lambda c, _t=tch: (x_ap(b, c, slice(_t * 512, (_t + 1) * 512)), True),
            "pre")
        if extra_cb is not None:
            extra_cb(tch, rows)
    b.fence(tag + "end", ("tensor", "vector", "scalar"))


def emit_store_x(b, final_fence):
    s_st = b.sem("store")
    for c in range(NDC):
        b.c.bump("store", 16)
        if b.on("sync"):
            if c == 0:
                b.E.wait_ge(b.sem("f_" + final_fence), 4)
            b.E.dma_start(out=b.t["x_out"].ap()[c * 128:(c + 1) * 128, :],
                          in_=x_ap(b, c)).then_inc(s_st, 16)
    if b.on("sync"):
        b.E.wait_ge(s_st, b.c.get("store"))
        if b.c.get("dbg"):
            b.E.wait_ge(b.sem("dbg"), b.c.get("dbg"))



ATT_HPS = list(range(8))
SERIAL = 0
COLDBLOCK = 0
ATT_TRUNC = None


def emit_attention(b, L):
    lt = "a%d" % L
    wqkv = b.t["l0_wqkv" if L == 0 else "l1_wqkv"].ap()
    psA, psB = b.t["psA"], b.t["psB"]
    o_sb = b.carve(AOF["o"], 128, 8 * T, BF16)
    q_hp = b.carve(AOF["q"], 128, T, BF16)
    k_hp = b.carve(AOF["k"], 128, T, BF16)
    v_hp = b.carve(AOF["v"], 128, T, BF16)
    p_sb = [b.carve(AOF["p"], 128, T, BF16), b.carve(AOF["p"] + 4096, 128, T, BF16)]
    pt_sb = b.carve(AOF["pt"], 128, 16 * 512, BF16)
    bw_sb = b.carve(AOF["bw"], 128, 4096, BF16)
    cos_sb = b.carve(AOF["bw"], 128, T, BF16)
    sin_sb = b.carve(AOF["bw"] + 4096, 128, T, BF16)
    vslab = b.carve(LN_OFF + 4096, 128, 8 * 128, BF16)
    rt1 = b.carve(LN_OFF, 128, 512, F32)
    rt2 = b.carve(LN_OFF + 2048, 128, 512, F32)
    negmax = b.t["negmax"]
    sums = b.t["sums"]
    recip = b.t["recip"]
    diag = b.t["diag"]

    s_w = b.sem(lt + "w")
    s_qk = b.sem(lt + "qk")
    s_qe = b.sem(lt + "qe")
    s_sw = b.sem(lt + "sw")
    s_rp = b.sem(lt + "rp")
    s_ve = b.sem(lt + "ve")
    s_Sm = b.sem(lt + "Sm")
    s_mx = b.sem(lt + "mx")
    s_ex = b.sem(lt + "ex")
    s_rc = b.sem(lt + "rc")
    s_dg = b.sem(lt + "dg")
    s_pt = b.sem(lt + "pt")
    s_pe = b.sem(lt + "pe")
    s_av = b.sem(lt + "av")
    s_ave = b.sem(lt + "ave")
    s_bw = b.sem(lt + "bw")

    if L == 0:
        for nm, ap_ in (("c_cos2", cos_sb), ("c_sin2", sin_sb)):
            b.c.bump(lt + "w", 16)
            if b.on("gpsimd"):
                b.E.dma_start(out=ap_, in_=b.t[nm].ap()[:, :]).then_inc(s_w, 16)

    def slab(base, dc):
        return b.carve(base + dc * 1024, 128, 512, BF16)

    for hp in (() if ATT_TRUNC == "none" else ATT_HPS):
        g = hp // 4
        if hp % 4 == 0:
            for fs, base in ((g, AOF["wsA"]), (2 + g, AOF["wsB"])):
                for dc in range(NDC):
                    b.c.bump(lt + "w", 16)
                    if b.on("gpsimd"):
                        if hp == 4 and dc == 0 and base == AOF["wsA"]:
                            b.E.wait_ge(s_qk, b.c.get(lt + "qkhp3"))
                        b.E.dma_start(out=slab(base, dc),
                                      in_=wqkv[dc, fs]).then_inc(s_w, 16)
        vfs = 4 + g
        vcol = (hp % 4) * 128
        for dc in range(NDC):
            b.c.bump(lt + "w", 16)
            if b.on("gpsimd"):
                if dc == 0:
                    if hp == ATT_HPS[0]:
                        b.E.wait_ge(b.sem(("ln0" if L == 0 else "ln4") + "n"),
                                    NDC * NTC)
                    else:
                        b.E.wait_ge(s_qk, b.c.get(lt + "qkhp%d" % (hp - 1)))
                b.E.dma_start(out=vslab[:, dc * 128:(dc + 1) * 128],
                              in_=wqkv[dc, vfs][:, vcol:vcol + 128]).then_inc(s_w, 16)
        n_w = b.c.get(lt + "w")

        qcol = (hp % 4) * 128
        for tch in range(NTC):
            if ATT_TRUNC == "w":
                continue
            sl = slice(tch * 512, (tch + 1) * 512)
            nqk = b.c.bump(lt + "qk")
            if b.on("tensor"):
                if tch == 0:
                    b.E.wait_ge(s_w, n_w)
                    if hp > 0:
                        b.E.wait_ge(s_pe, b.c.get(lt + "pe"))
                        b.E.wait_ge(s_ave, b.c.get(lt + "ave"))
                elif L == 1:
                    b.E.wait_ge(s_qe, b.c.get(lt + "qe"))
                for dc in range(NDC):
                    b.E.matmul(psA[:, 0:512], slab(AOF["wsA"], dc)[:, qcol:qcol + 128],
                               xln_ap(b, dc, sl), start=(dc == 0), stop=(dc == NDC - 1))
                ins = None
                for dc in range(NDC):
                    ins = b.E.matmul(psA[:, 512:1024],
                                     slab(AOF["wsB"], dc)[:, qcol:qcol + 128],
                                     xln_ap(b, dc, sl), start=(dc == 0),
                                     stop=(dc == NDC - 1))
                ins.then_inc(s_qk, 1)
            nev = b.c.bump(lt + "qe")
            if b.on("scalar"):
                b.E.wait_ge(s_qk, nqk)
                if L == 0:
                    b.E.activation(out=q_hp[:, sl], in_=psA[:, 0:512],
                                   func=AF.Identity, scale=0.125)
                    b.E.activation(out=k_hp[:, sl], in_=psA[:, 512:1024],
                                   func=AF.Identity).then_inc(s_qe, 1)
                else:
                    b.E.activation(out=q_hp[:, sl], in_=psA[:, 0:512],
                                   func=AF.Identity, scale=0.125,
                                   bias=b.t["bq8_col"][:, hp:hp + 1])
                    b.E.activation(out=k_hp[:, sl], in_=psA[:, 512:1024],
                                   func=AF.Identity,
                                   bias=b.t["bk_col"][:, hp:hp + 1]).then_inc(s_qe, 1)
            if L == 0:
                nsw = b.c.bump(lt + "sw")
                if b.on("tensor"):
                    b.E.wait_ge(s_qe, nev)
                    if b.c.get(lt + "rp"):
                        b.E.wait_ge(s_rp, b.c.get(lt + "rp"))
                    b.E.matmul(psA[:, 1536:2048], b.t["swap_bf"][:, :], q_hp[:, sl],
                               start=True, stop=True)
                    b.E.matmul(psA[:, 1024:1536], b.t["swap_bf"][:, :], k_hp[:, sl],
                               start=True, stop=True).then_inc(s_sw, 1)
                b.c.bump(lt + "rp")
                if b.on("vector"):
                    b.E.wait_ge(s_sw, nsw)
                    b.E.tensor_tensor(rt1, q_hp[:, sl], cos_sb[:, sl], OP.mult)
                    b.E.tensor_tensor(rt2, psA[:, 1536:2048], sin_sb[:, sl], OP.mult)
                    b.E.tensor_tensor(q_hp[:, sl], rt1, rt2, OP.add)
                    b.E.tensor_tensor(rt1, k_hp[:, sl], cos_sb[:, sl], OP.mult)
                    b.E.tensor_tensor(rt2, psA[:, 1024:1536], sin_sb[:, sl], OP.mult)
                    b.E.tensor_tensor(k_hp[:, sl], rt1, rt2, OP.add).then_inc(s_rp, 1)
            if ATT_TRUNC == "qk":
                continue
            nqk2 = b.c.bump(lt + "qk")
            if b.on("tensor"):
                if tch >= 1:
                    b.E.wait_ge(s_ve, b.c.get(lt + "ve"))
                ins = None
                for ti in range(4):
                    tt = tch * 4 + ti
                    for dc in range(NDC):
                        ins = b.E.matmul(psB[:, ti * 512:ti * 512 + 128],
                                         xln_ap(b, dc, slice(tt * 128, (tt + 1) * 128)),
                                         vslab[:, dc * 128:(dc + 1) * 128],
                                         start=(dc == 0), stop=(dc == NDC - 1))
                ins.then_inc(s_qk, 1)
            b.c.bump(lt + "ve")
            if b.on("scalar"):
                b.E.wait_ge(s_qk, nqk2)
                ins = None
                for ti in range(4):
                    tt = tch * 4 + ti
                    ins = b.E.activation(out=v_hp[:, tt * 128:(tt + 1) * 128],
                                         in_=psB[:, ti * 512:ti * 512 + 128],
                                         func=AF.Identity)
                ins.then_inc(s_ve, 1)
        b.c.v[lt + "qkhp%d" % hp] = b.c.get(lt + "qk")
        n_rope = b.c.get(lt + "rp")
        n_vev = b.c.get(lt + "ve")
        n_qev = b.c.get(lt + "qe")
        if hp == 0:
            b.dump(lt + "q", q_hp, b.sem(lt + ("rp" if L == 0 else "qe")),
                   n_rope if L == 0 else n_qev)
            b.dump(lt + "k", k_hp, b.sem(lt + ("rp" if L == 0 else "qe")),
                   n_rope if L == 0 else n_qev)
            b.dump(lt + "v", v_hp, b.sem(lt + "ve"), n_vev)

        if COLDBLOCK:
            if b.on("vector", "scalar", "gpsimd"):
                if L == 0:
                    b.E.wait_ge(s_rp, n_rope)
                b.E.wait_ge(s_qe, n_qev)
                b.E.wait_ge(s_ve, n_vev)
        if ATT_TRUNC in ("w", "qk", "qkv"):
            continue
        for hh in range(2):
            h = 2 * hp + hh
            qrow = hh * 64
            if L == 1:
                b.c.bump(lt + "bw", 16)
                if b.on("sync"):
                    if h >= 1:
                        b.E.wait_ge(s_Sm, b.c.get(lt + "Sm"))
                    src = bass.AP(b.t["exp_dram"], h * 4352, [[1, 128], [1, 4096]])
                    b.E.dma_start(out=bw_sb, in_=src).then_inc(s_bw, 16)
            if ATT_TRUNC == "bw":
                continue
            for qt in range(NQT):
                ti_g = b.c.bump(lt + "tile") - 1
                par = ti_g % 2
                grp = psA if par == 0 else psB
                nblk = NTC if L == 1 else (qt // 4 + 1)
                span = nblk * 512
                qs = qt // 4
                nS = b.c.bump(lt + "Sm")
                if b.on("tensor"):
                    if SERIAL and ti_g >= 1:
                        b.E.wait_ge(s_pe, ti_g)
                        b.E.wait_ge(s_ave, b.c.get(lt + "ave"))
                    if qt == 0:
                        if L == 0:
                            b.E.wait_ge(s_rp, n_rope)
                        else:
                            b.E.wait_ge(s_qe, n_qev)
                            b.E.wait_ge(s_bw, b.c.get(lt + "bw"))
                        b.E.wait_ge(s_ve, n_vev)
                    if ti_g >= 2:
                        b.E.wait_ge(s_pe, ti_g - 1)
                        b.E.wait_ge(s_ave, b.c.get(lt + "ave"))
                    last = {}
                    for nb in range(nblk):
                        last[nb] = ("S", nb)
                    if L == 1:
                        for nb in range(nblk):
                            last[nb] = ("B", nb)
                    else:
                        last[qt * 128 // 512] = ("M", qt)
                        for kk in range(qt + 1, nblk * 4):
                            last[kk // 512] = ("F", kk)
                    ins = None
                    for nb in range(nblk):
                        ins = b.E.matmul(grp[:, nb * 512:(nb + 1) * 512],
                                         q_hp[qrow:qrow + 64, qt * 128:(qt + 1) * 128],
                                         k_hp[qrow:qrow + 64, nb * 512:(nb + 1) * 512],
                                         start=True, stop=(last[nb] == ("S", nb)),
                                         skip_group_check=True)
                    if L == 1:
                        for nb in range(nblk):
                            w0 = 2048 + nb * 512 - 128 * qt
                            ins = b.E.matmul(grp[:, nb * 512:(nb + 1) * 512],
                                             b.t["irev_bf"][:, :],
                                             bw_sb[:, w0:w0 + 512],
                                             start=False, stop=True,
                                             skip_group_check=True)
                    else:
                        dg = qt * 128
                        ins = b.E.matmul(grp[:, dg:dg + 128], b.t["iden_bf"][:, :],
                                         b.t["cmask_bf"][:, :], start=False,
                                         stop=(last[dg // 512] == ("M", qt)),
                                         skip_group_check=True)
                        for kk in range(qt + 1, nblk * 4):
                            ins = b.E.matmul(grp[:, kk * 128:(kk + 1) * 128],
                                             b.t["iden_bf"][:, :],
                                             b.t["negf_bf"][:, :], start=False,
                                             stop=(last[kk // 512] == ("F", kk)),
                                             skip_group_check=True)
                    ins.then_inc(s_Sm, 1)
                nmx = b.c.bump(lt + "mx")
                if b.on("vector"):
                    b.E.wait_ge(s_Sm, nS)
                    if ti_g >= 2:
                        b.E.wait_ge(s_ex, ti_g - 1)
                    b.E.reduce_max(negmax[:, par:par + 1], grp[:, 0:span], axis=AX.X,
                                   negate=True).then_inc(s_mx, 1)
                nex = b.c.bump(lt + "ex")
                if b.on("scalar"):
                    b.E.wait_ge(s_mx, nmx)
                    if ti_g >= 2:
                        b.E.wait_ge(s_pt, ti_g - 1)
                    b.E.activation(out=p_sb[par][:, 0:span], in_=grp[:, 0:span],
                                   func=AF.Exp, bias=negmax[:, par:par + 1],
                                   accum_out=sums[:, par:par + 1]).then_inc(s_ex, 1)
                nrc = b.c.bump(lt + "rc")
                ndg = b.c.bump(lt + "dg")
                if b.on("vector"):
                    b.E.wait_ge(s_ex, nex)
                    if ndg > 2:
                        b.E.wait_ge(s_dg, ndg - 2)
                    b.E.reciprocal(recip[:, par:par + 1],
                                   sums[:, par:par + 1]).then_inc(s_rc, 1)
                neng = "vector" if par == 0 else "gpsimd"
                if b.on(neng):
                    b.E.wait_ge(s_rc, nrc)
                    b.E.tensor_scalar_mul(p_sb[par][:, 0:span], p_sb[par][:, 0:span],
                                          recip[:, par:par + 1]).then_inc(s_dg, 1)
                npt = b.c.bump(lt + "pt")
                if b.on("tensor"):
                    b.E.wait_ge(s_dg, ndg)
                    ptv = grp[:, 0:1024].bitcast(BF16)
                    ins = None
                    for sc in range(nblk * 4):
                        ins = b.E.transpose(ptv[:, sc * 128:(sc + 1) * 128],
                                            p_sb[par][:, sc * 128:(sc + 1) * 128],
                                            b.t["iden_bf"][:, :])
                    ins.then_inc(s_pt, 1)
                b.c.bump(lt + "pe")
                if b.on("vector"):
                    b.E.wait_ge(s_pt, npt)
                    if qt % 4 == 0 and b.c.get(lt + "av") > 0:
                        b.E.wait_ge(s_av, b.c.get(lt + "av"))
                    ptv = grp[:, 0:1024].bitcast(BF16)
                    dst = pt_sb.rearrange("a (s q) -> a s q", q=512)[
                        :, 0:nblk * 4, (qt % 4) * 128:(qt % 4) * 128 + 128]
                    srcv = ptv[:, 0:nblk * 4 * 128].rearrange("a (s q) -> a s q", q=128)
                    b.E.tensor_copy(dst, srcv).then_inc(s_pe, 1)
                if ti_g == 1:
                    b.dump(lt + "p1", p_sb[1], s_dg, b.c.get(lt + "dg"))
                if qt % 4 == 3:
                    nav = b.c.bump(lt + "av")
                    if b.on("tensor"):
                        b.E.wait_ge(s_pe, b.c.get(lt + "pe"))
                        b.E.wait_ge(s_ave, b.c.get(lt + "ave"))
                        ot = grp[qrow:qrow + 64, 1024:1536]
                        ins = None
                        for sc in range(nblk * 4):
                            ins = b.E.matmul(
                                ot, v_hp[:, sc * 128 + qrow: sc * 128 + qrow + 64],
                                pt_sb[:, sc * 512: sc * 512 + 512],
                                start=(sc == 0), stop=(sc == nblk * 4 - 1),
                                tile_position=(0, qrow))
                        ins.then_inc(s_av, 1)
                    b.c.bump(lt + "ave")
                    if b.on("scalar"):
                        b.E.wait_ge(s_av, nav)
                        dst = b.carve(AOF["o"] + hp * 2 * T + qs * 1024, 128, 512, BF16)
                        b.E.activation(out=dst[qrow:qrow + 64, :],
                                       in_=grp[qrow:qrow + 64, 1024:1536],
                                       func=AF.Identity).then_inc(s_ave, 1)
                    if ti_g == 3:
                        b.dump(lt + "pt1", pt_sb, s_av, b.c.get(lt + "av"))
                    if ti_g == 15:
                        b.dump(lt + "oe", o_sb, s_ave, b.c.get(lt + "ave"))

    b.dump(lt + "o", o_sb, s_ave, b.c.get(lt + "ave"))
    b.dump(lt + "p", p_sb[1], s_ex, b.c.get(lt + "ex"))
    b.dump(lt + "pt", pt_sb, s_av, b.c.get(lt + "av"))

    if ATT_TRUNC is not None:
        if b.on("sync") and b.c.get(lt + "bw"):
            b.E.wait_ge(s_bw, b.c.get(lt + "bw"))
        if b.on("gpsimd") and b.c.get(lt + "w"):
            b.E.wait_ge(s_w, b.c.get(lt + "w"))
        b.fence(lt + "done")
        return

    wout_d = b.t["l0_wout" if L == 0 else "l1_wd"].ap()
    blkout = b.carve(AOF["q"], 128, 8 * 512, BF16)
    s_pw = b.sem(lt + "pw")
    s_pj = b.sem(lt + "pj")
    s_pv = b.sem(lt + "pv")

    def wout_sb(dc):
        return b.carve(AOF["wsA"] + dc * 2048, 128, 1024, BF16)

    for dc in range(NDC):
        for fs in range(2):
            b.c.bump(lt + "pw", 16)
            if b.on("gpsimd"):
                if dc == 0 and fs == 0:
                    b.E.wait_ge(s_qk, b.c.get(lt + "qkhp7"))
                b.E.dma_start(out=b.carve(AOF["wsA"] + dc * 2048 + fs * 1024, 128, 512, BF16),
                              in_=wout_d[dc, fs]).then_inc(s_pw, 16)
    npw = b.c.get(lt + "pw")

    if L == 1:
        s_pb = b.sem(lt + "pb")
        npb0 = b.c.bump(lt + "pb")
        if b.on("vector"):
            b.E.tensor_copy(b.t["bvb_col"][:, :], b.t["bv_col"][:, :]).then_inc(s_pb, 1)
        for ft in range(8):
            npbm = b.c.bump(lt + "pbm")
            if b.on("tensor"):
                if ft == 0:
                    b.E.wait_ge(s_pw, npw)
                    b.E.wait_ge(s_pb, npb0)
                    b.E.wait_ge(s_pe, b.c.get(lt + "pe"))
                    b.E.wait_ge(s_ave, b.c.get(lt + "ave"))
                if ft >= 2:
                    b.E.wait_ge(s_pb, b.c.get(lt + "pb") - 1)
                ins = None
                slot = 1024 + (ft % 2) * 512
                for dc in range(NDC):
                    ins = b.E.matmul(psB[:, slot:slot + 1],
                                     wout_sb(dc)[:, ft * 128:(ft + 1) * 128],
                                     b.t["bvb_col"][:, dc:dc + 1],
                                     start=(dc == 0), stop=(dc == NDC - 1))
                ins.then_inc(b.sem(lt + "pbm"), 1)
            b.c.bump(lt + "pb")
            if b.on("vector"):
                b.E.wait_ge(b.sem(lt + "pbm"), npbm)
                b.E.tensor_copy(b.t["pb_col"][:, ft:ft + 1],
                                psB[:, 1024 + (ft % 2) * 512:1025 + (ft % 2) * 512]).then_inc(s_pb, 1)
        b.c.bump(lt + "pb")
        if b.on("vector"):
            b.E.wait_ge(s_pb, b.c.get(lt + "pb") - 1)
            b.E.tensor_add(b.t["pb_col"][:, :], b.t["pb_col"][:, :],
                           b.t["bd_col"][:, :]).then_inc(s_pb, 1)
        npb_all = b.c.get(lt + "pb")

    for tch in range(NTC):
        grp = psA if tch % 2 == 0 else psB
        for ft in range(NDC):
            npj = b.c.bump(lt + "pj")
            if b.on("tensor"):
                if tch == 0 and ft == 0:
                    b.E.wait_ge(s_pw, npw)
                    b.E.wait_ge(s_pe, b.c.get(lt + "pe"))
                    b.E.wait_ge(s_ave, b.c.get(lt + "ave"))
                    if L == 1:
                        b.E.wait_ge(b.sem(lt + "pb"), npb_all)
                if npj > 2:
                    b.E.wait_ge(s_pv, npj - 2)
                pslot = grp[:, (ft % 2) * 512:(ft % 2) * 512 + 512]
                ins = None
                for dc in range(NDC):
                    ins = b.E.matmul(pslot, wout_sb(dc)[:, ft * 128:(ft + 1) * 128],
                                     o_sb[:, dc * T + tch * 512: dc * T + (tch + 1) * 512],
                                     start=(dc == 0), stop=(dc == NDC - 1))
                ins.then_inc(s_pj, 1)
            b.c.bump(lt + "pv")
            if b.on("scalar"):
                b.E.wait_ge(s_pj, npj)
                pslot = grp[:, (ft % 2) * 512:(ft % 2) * 512 + 512]
                if L == 1:
                    b.E.activation(out=blkout[:, ft * 512:(ft + 1) * 512], in_=pslot,
                                   func=AF.Identity,
                                   bias=b.t["pb_col"][:, ft:ft + 1]).then_inc(s_pv, 1)
                else:
                    b.E.activation(out=blkout[:, ft * 512:(ft + 1) * 512], in_=pslot,
                                   func=AF.Identity).then_inc(s_pv, 1)
        if tch == 0:
            b.dump(lt + "bo", blkout, s_pv, b.c.get(lt + "pv"))
            if L == 1:
                b.dump(lt + "pb", b.t["pb_col"][:, :], s_pv, b.c.get(lt + "pv"))
        emit_ln_tch(b, lt + "pl", tch,
                    lambda c: (blkout[:, c * 512:(c + 1) * 512], False),
                    "post", guard=(s_pv, b.c.get(lt + "pv")))
    b.fence(lt + "done")



def emit_relpos(b):
    s_ld = b.sem("rpl")
    s_t = b.sem("rpt")
    s_v = b.sem("rpv")
    s_e = b.sem("rpe")
    s_d = b.sem("rpd")
    tbl_sb = b.carve(AOF["pt"], 16, 32, F32)
    tblT = b.carve(AOF["pt"] + 512, 32, 16, BF16)
    oht = b.carve(AOF["pt"] + 1024, 32, 4096, BF16)
    exps = b.carve(AOF["pt"] + 10 * 1024, 16, 512, BF16)
    zrow = b.carve(AOF["pt"] + 12 * 1024, 16, 128, BF16)
    ed = b.t["exp_dram"]
    iden16 = b.t["iden16"]

    s_lm = b.sem("rplm")
    b.c.bump("rpl", 32)
    b.c.bump("rplm", 1)
    if b.on("sync"):
        b.E.dma_start(out=tbl_sb, in_=b.t["l1_tbl"].ap()[:, :]).then_inc(s_ld, 16)
    if b.on("gpsimd"):
        b.E.dma_start(out=oht, in_=b.t["c_oht"].ap()[:, :]).then_inc(s_ld, 16)
        b.E.memset(zrow, 0.0)
        b.E.nop().then_inc(s_lm, 1)
    n0 = b.c.get("rpl")
    nt = b.c.bump("rpt")
    if b.on("tensor"):
        b.E.wait_ge(s_ld, n0)
        b.E.wait_ge(s_lm, 1)
        b.E.wait_ge(b.sem("f_bcols"), 3)
        b.E.transpose(b.t["psA"][0:32, 0:16], tbl_sb[0:16, 0:32],
                      iden16[0:16, 0:16]).then_inc(s_t, 1)
    nv = b.c.bump("rpv")
    if b.on("vector"):
        b.E.wait_ge(s_t, nt)
        b.E.tensor_copy(tblT[0:32, :], b.t["psA"][0:32, 0:16]).then_inc(s_v, 1)
    for ch in range(8):
        nt = b.c.bump("rpt")
        if b.on("tensor"):
            if ch == 0:
                b.E.wait_ge(s_v, nv)
            if ch >= 2:
                b.E.wait_ge(s_e, b.c.get("rpe") - 1)
            b.E.matmul(b.t["psB"][0:16, (ch % 2) * 512:(ch % 2) * 512 + 512],
                       tblT[0:32, :], oht[0:32, ch * 512:(ch + 1) * 512],
                       start=True, stop=True).then_inc(s_t, 1)
        ne = b.c.bump("rpe")
        if b.on("scalar"):
            b.E.wait_ge(s_t, nt)
            if ch >= 1:
                b.E.wait_ge(s_d, b.c.get("rpd"))
            b.E.activation(out=exps[0:16, :],
                           in_=b.t["psB"][0:16, (ch % 2) * 512:(ch % 2) * 512 + 512],
                           func=AF.Identity).then_inc(s_e, 1)
        b.c.bump("rpd", 16)
        if b.on("sync"):
            b.E.wait_ge(s_e, ne)
            b.E.dma_start(out=bass.AP(ed, 128 + ch * 512, [[4352, 16], [1, 512]]),
                          in_=exps[0:16, :]).then_inc(s_d, 16)
    b.c.bump("rpd", 32)
    if b.on("sync"):
        b.E.wait_ge(s_lm, 1)
        b.E.dma_start(out=bass.AP(ed, 0, [[4352, 16], [1, 128]]),
                      in_=zrow).then_inc(s_d, 16)
        b.E.dma_start(out=bass.AP(ed, 4224, [[4352, 16], [1, 128]]),
                      in_=zrow).then_inc(s_d, 16)
        b.E.wait_ge(s_d, b.c.get("rpd"))
    b.fence("rpdone", ("tensor", "vector", "scalar"))



def rk1_16(b):
    return b.t["rk116"][:, :]


def emit_mod(b):
    psA, psB = b.t["psA"], b.t["psB"]
    w1 = b.t["l0_w1"].ap()
    w2 = b.t["l0_w2"].ap()
    sc_dram = b.t["scores_dram"]
    wf_dram = b.t["wf_dram"]
    h_dram = b.t["h_dram"]
    blk_dram = b.t["blk_dram"]
    srow = b.carve(86 * 1024, 1, 512, F32)
    wr_sb = b.carve(88 * 1024, 128, 8, F32)
    sw_sb = b.t["sw11"]

    s_mdw = b.sem("mdw")
    s_swm = b.sem("mdsw")
    b.c.bump("mdw", 16)
    if b.on("sync"):
        b.E.dma_start(out=wr_sb, in_=b.t["l0_wr"].ap()[:, :]).then_inc(s_mdw, 16)
    nsw = b.c.bump("mdsw")
    if b.on("tensor"):
        b.E.wait_ge(s_mdw, 16)
        if b.has_l0a:
            b.E.wait_ge(b.sem("f_a0done"), 4)
        ins = None
        for dc in range(NDC):
            ins = b.E.matmul(psB[0:1, 512:513], wr_sb[:, dc:dc + 1],
                             b.t["onescol_f"][:, 0:1], start=(dc == 0),
                             stop=(dc == NDC - 1))
        ins.then_inc(s_swm, 1)
    b.c.bump("mdsw")
    if b.on("vector"):
        b.E.wait_ge(s_swm, nsw)
        b.E.tensor_copy(sw_sb[0:1, 0:1], psB[0:1, 512:513]).then_inc(s_swm, 1)

    def scores_cb(tch, rows):
        grp = psA if tch % 2 == 0 else psB
        sl = slice(tch * 512, (tch + 1) * 512)
        nm = b.c.bump("mdscm")
        if b.on("tensor"):
            ins = None
            for dc in range(NDC):
                ins = b.E.matmul(grp[0:1, 0:512], wr_sb[:, dc:dc + 1],
                                 x_ap(b, dc, sl), start=(dc == 0),
                                 stop=(dc == NDC - 1))
            ins.then_inc(b.sem("mdscm"), 1)
        b.c.bump("mdscv")
        if b.on("vector"):
            b.E.wait_ge(b.sem("mdscm"), nm)
            b.E.wait_ge(s_swm, 2)
            if tch >= 1:
                b.E.wait_ge(b.sem("mdscd"), b.c.get("mdscd"))
            b.E.tensor_scalar(srow, rows[0:1, R_M], sw_sb[0:1, 0:1], None, OP.mult)
            b.E.tensor_tensor(srow, grp[0:1, 0:512], srow, OP.subtract)
            b.E.tensor_tensor(srow, srow, rows[0:1, R_RSTD],
                              OP.mult).then_inc(b.sem("mdscv"), 1)
        b.c.bump("mdscd", 16)
        if b.on("sync"):
            b.E.wait_ge(b.sem("mdscv"), b.c.get("mdscv"))
            b.E.dma_start(out=sc_dram.ap()[tch * 512:(tch + 1) * 512]
                          .rearrange("(a n) -> a n", a=1),
                          in_=srow).then_inc(b.sem("mdscd"), 16)

    emit_pre_ln(b, "ln2", extra_cb=scores_cb)

    sp = b.t["scores_p"]
    ranks = b.t["ranks"]
    rk1 = b.t["rk1"]
    gt = b.carve(72 * 1024, 128, 512, F32)
    wfp = b.t["wfull_p"]
    b.c.bump("mdrd", 16)
    if b.on("sync"):
        b.E.wait_ge(b.sem("mdscd"), b.c.get("mdscd"))
        b.E.dma_start(out=sp[:, :],
                      in_=sc_dram.ap().rearrange("(p f) -> p f", p=128)).then_inc(
            b.sem("mdrd"), 16)
    if b.on("vector"):
        b.E.memset(ranks[:, :], 0.0)
        b.E.wait_ge(b.sem("mdrd"), 16)
    for rch in range(4):
        b.c.bump("mdrr", 16)
        if b.on("sync"):
            if rch >= 1:
                b.E.wait_ge(b.sem("mdrb"), b.c.get("mdrb"))
            b.E.dma_start(out=srow, in_=sc_dram.ap()[rch * 512:(rch + 1) * 512]
                          .rearrange("(a n) -> a n", a=1)).then_inc(b.sem("mdrr"), 16)
        nb_ = b.c.bump("mdrb")
        if b.on("tensor"):
            b.E.wait_ge(b.sem("mdrr"), b.c.get("mdrr"))
            if rch >= 1:
                b.E.wait_ge(b.sem("mdrv"), b.c.get("mdrv"))
            b.E.matmul(psA[:, 0:512], b.t["onesrow_f"][0:1, :], srow,
                       start=True, stop=True).then_inc(b.sem("mdrb"), 1)
        b.c.bump("mdrv")
        if b.on("vector"):
            b.E.wait_ge(b.sem("mdrb"), nb_)
            ins = None
            for f in range(16):
                ins = b.E.tensor_scalar(gt, psA[:, 0:512], sp[:, f:f + 1], 0.0,
                                        OP.is_gt, OP.add,
                                        accum_out=b.t["rksc"][:, rch * 16 + f:rch * 16 + f + 1])
            ins.then_inc(b.sem("mdrv"), 1)
    b.c.bump("mdrs")
    if b.on("scalar"):
        b.E.wait_ge(b.sem("mdrd"), 16)
        b.E.activation(out=wfp[:, :], in_=sp[:, :],
                       func=AF.Sigmoid).then_inc(b.sem("mdrs"), 1)
    nv1 = b.c.bump("mdrv")
    nv2 = b.c.bump("mdrv")
    nv3 = b.c.bump("mdrv")
    if b.on("vector"):
        b.E.wait_ge(b.sem("mdrs"), b.c.get("mdrs"))
        b.E.wait_ge(b.sem("mdrv"), nv1 - 1)
        rks = b.t["rksc"]
        b.E.tensor_add(ranks[:, :], rks[:, 0:16], rks[:, 16:32])
        b.E.tensor_add(rk1_16(b), rks[:, 32:48], rks[:, 48:64]).then_inc(b.sem("mdrv"), 1)
        b.E.wait_ge(b.sem("mdrv"), nv1)
        b.E.tensor_add(ranks[:, :], ranks[:, :], rk1_16(b)).then_inc(b.sem("mdrv"), 1)
        b.E.wait_ge(b.sem("mdrv"), nv2)
        b.E.tensor_single_scalar(b.t["sel16"][:, :], ranks[:, :], 1024.0,
                                 OP.is_lt).then_inc(b.sem("mdrv"), 1)
        b.E.wait_ge(b.sem("mdrv"), nv3)
        b.E.tensor_tensor(wfp[:, :], wfp[:, :], b.t["sel16"][:, :],
                          OP.mult).then_inc(b.sem("mdrv"), 1)
    b.c.bump("mdrv")
    b.c.bump("mdwd", 16)
    if b.on("sync"):
        b.E.wait_ge(b.sem("mdrv"), b.c.get("mdrv"))
        b.E.dma_start(out=wf_dram.ap().rearrange("(p f) -> p f", p=128),
                      in_=wfp[:, :]).then_inc(b.sem("mdwd"), 16)

    b.dump("mdwf", wfp[:, :], b.sem("mdrv"), b.c.get("mdrv"))
    b.dump("mdsp", sp[:, :], b.sem("mdrd"), 16)
    b.dump("mdrk", b.t["sel16"][:, :], b.sem("mdrv"), b.c.get("mdrv"))

    hslot = [b.carve(80 * 1024 + i * 1024, 128, 512, BF16) for i in range(2)]
    for fs in range(8):
        wbuf = fs % 2
        for dc in range(NDC):
            b.c.bump("md1w", 16)
            if b.on("gpsimd"):
                if dc == 0 and fs >= 2:
                    b.E.wait_ge(b.sem("md1m"), (fs - 1) * 16)
                b.E.dma_start(out=b.carve(AOF["wsA"] + wbuf * 8192 + dc * 1024,
                                          128, 512, BF16),
                              in_=w1[dc, fs]).then_inc(b.sem("md1w"), 16)
        nw = b.c.get("md1w")
        for ft in range(4):
            gft = fs * 4 + ft
            for tch in range(NTC):
                grp = psA if tch % 2 == 0 else psB
                slot = (gft * NTC + tch) % 2
                nm = b.c.bump("md1m")
                if b.on("tensor"):
                    if ft == 0 and tch == 0:
                        b.E.wait_ge(b.sem("md1w"), nw)
                        if fs == 0:
                            b.E.wait_ge(b.sem("f_ln2end"), 3)
                            b.E.wait_ge(b.sem("mdrb"), 4)
                            b.E.wait_ge(b.sem("mdrv"), b.c.get("mdrv"))
                    if nm > 2:
                        b.E.wait_ge(b.sem("md1e"), nm - 2)
                    pslot = grp[:, (nm % 2) * 512:(nm % 2) * 512 + 512]
                    ins = None
                    for dc in range(NDC):
                        ins = b.E.matmul(pslot,
                                         b.carve(AOF["wsA"] + wbuf * 8192 + dc * 1024,
                                                 128, 512, BF16)[:, ft * 128:(ft + 1) * 128],
                                         xln_ap(b, dc, slice(tch * 512, (tch + 1) * 512)),
                                         start=(dc == 0), stop=(dc == NDC - 1))
                    ins.then_inc(b.sem("md1m"), 1)
                ne = b.c.bump("md1e")
                if b.on("scalar"):
                    b.E.wait_ge(b.sem("md1m"), nm)
                    if ne > 2:
                        b.E.wait_ge(b.sem("md1d"), 16 * (ne - 2))
                    pslot = grp[:, (nm % 2) * 512:(nm % 2) * 512 + 512]
                    b.E.activation(out=hslot[slot], in_=pslot,
                                   func=AF.Gelu_apprx_tanh,
                                   bias=b.t["b1_col"][:, gft:gft + 1]).then_inc(
                        b.sem("md1e"), 1)
                b.c.bump("md1d", 16)
                if b.on("sync"):
                    b.E.wait_ge(b.sem("md1e"), ne)
                    b.E.dma_start(out=h_dram.ap()[gft * 128:(gft + 1) * 128,
                                                  tch * 512:(tch + 1) * 512],
                                  in_=hslot[slot]).then_inc(b.sem("md1d"), 16)

    hstr = b.carve(AOF["wsA"], 128, 32 * 512, BF16)
    wfsb = b.carve(64 * 1024, 128, 512, F32)
    ytmp = b.carve(66 * 1024, 128, 512, F32)
    utmp = b.carve(68 * 1024, 128, 512, F32)
    ubf = b.carve(70 * 1024, 128, 512, BF16)

    def w2h(kc):
        return b.carve(kc * 1024, 128, 512, BF16)

    n_md1 = 8 * 4 * NTC
    for dh in range(2):
        for kc in range(32):
            b.c.bump("md2w", 16)
            if b.on("gpsimd"):
                if kc == 0:
                    b.E.wait_ge(b.sem("md1m"), n_md1)
                    if dh == 1:
                        b.E.wait_ge(b.sem("md2m"), 16)
                b.E.dma_start(out=w2h(kc), in_=w2[kc, dh]).then_inc(b.sem("md2w"), 16)
        nw2 = b.c.get("md2w")
        for tch in range(NTC):
            grp = psA if tch % 2 == 0 else psB
            sl = slice(tch * 512, (tch + 1) * 512)
            for kc in range(32):
                b.c.bump("md2h", 16)
                if b.on("sync"):
                    if kc == 0:
                        b.E.wait_ge(b.sem("md2g"), b.c.get("md2g"))
                        if dh == 0 and tch == 0:
                            b.E.wait_ge(b.sem("md1d"), 16 * n_md1)
                        if dh == 0 and tch == 0:
                            b.E.wait_ge(b.sem("md1m"), n_md1)
                    b.E.dma_start(out=hstr[:, kc * 512:(kc + 1) * 512],
                                  in_=h_dram.ap()[kc * 128:(kc + 1) * 128, sl]).then_inc(
                        b.sem("md2h"), 16)
            nh = b.c.get("md2h")
            nwb = b.c.bump("md2wr", 16)
            if b.on("sync"):
                b.E.wait_ge(b.sem("md2wv"), b.c.get("md2wv"))
                b.E.dma_start(out=srow, in_=wf_dram.ap()[tch * 512:(tch + 1) * 512]
                              .rearrange("(a n) -> a n", a=1)).then_inc(b.sem("md2wr"), 16)
            nwbm = b.c.bump("md2wb")
            if b.on("tensor"):
                b.E.wait_ge(b.sem("md2wr"), b.c.get("md2wr"))
                b.E.wait_ge(b.sem("mdwd"), 16)
                if b.c.get("md2wv") >= 1:
                    b.E.wait_ge(b.sem("md2wv"), b.c.get("md2wv"))
                b.E.matmul(grp[:, 1024:1536], b.t["onesrow_f"][0:1, :], srow,
                           start=True, stop=True).then_inc(b.sem("md2wb"), 1)
            b.c.bump("md2wv")
            if b.on("scalar"):
                b.E.wait_ge(b.sem("md2wb"), nwbm)
                b.E.activation(out=wfsb, in_=grp[:, 1024:1536],
                               func=AF.Identity).then_inc(b.sem("md2wv"), 1)
            for j in range(4):
                dt = dh * 4 + j
                nm = b.c.bump("md2m")
                if b.on("tensor"):
                    if j == 0:
                        b.E.wait_ge(b.sem("md2w"), nw2)
                        b.E.wait_ge(b.sem("md2h"), nh)
                    if nm > 2:
                        b.E.wait_ge(b.sem("md2e"), nm - 2)
                    pslot = grp[:, (nm % 2) * 512:(nm % 2) * 512 + 512]
                    ins = None
                    for kc in range(32):
                        ins = b.E.matmul(pslot, w2h(kc)[:, j * 128:(j + 1) * 128],
                                         hstr[:, kc * 512:(kc + 1) * 512],
                                         start=(kc == 0), stop=(kc == 31))
                    ins.then_inc(b.sem("md2m"), 1)
                ne = b.c.bump("md2e")
                if b.on("scalar"):
                    b.E.wait_ge(b.sem("md2m"), nm)
                    if ne > 1:
                        b.E.wait_ge(b.sem("md2g"), ne - 1)
                    pslot = grp[:, (nm % 2) * 512:(nm % 2) * 512 + 512]
                    b.E.activation(out=ytmp, in_=pslot, func=AF.Identity,
                                   bias=b.t["b2_col"][:, dt:dt + 1]).then_inc(
                        b.sem("md2e"), 1)
                ng = b.c.bump("md2g")
                if b.on("gpsimd"):
                    b.E.wait_ge(b.sem("md2e"), ne)
                    b.E.wait_ge(b.sem("md2wv"), b.c.get("md2wv"))
                    if ng > 1:
                        b.E.wait_ge(b.sem("md2bd"), 16 * (ng - 1))
                    b.E.tensor_tensor(utmp, ytmp, wfsb, OP.mult)
                    b.E.tensor_tensor(ubf, utmp,
                                      xln_ap(b, dt, sl), OP.add).then_inc(b.sem("md2g"), 1)
                b.c.bump("md2bd", 16)
                if b.on("sync"):
                    b.E.wait_ge(b.sem("md2g"), ng)
                    b.E.dma_start(out=blk_dram.ap()[dt * 128:(dt + 1) * 128, sl],
                                  in_=ubf).then_inc(b.sem("md2bd"), 16)

    if "mdblk" in b.dbg:
        b.c.bump("dbg", 16)
        if b.on("sync"):
            b.E.wait_ge(b.sem("md2bd"), b.c.get("md2bd"))
            b.E.dma_start(out=bass.AP(b.t["dbg_mdblk"], 0, [[2048, 1024], [1, 2048]]),
                          in_=blk_dram.ap()[:, :]).then_inc(b.sem("dbg"), 16)
    emit_postln_from_dram(b, "ln3", blk_dram, "md2bd")


def emit_postln_from_dram(b, tag, blk_dram, dsem_name):
    blkout = b.carve(AOF["q"], 128, 8 * 512, BF16)
    ldeng = "gpsimd" if blk_dram.dtype == F32 else "sync"
    for tch in range(NTC):
        for c in range(NDC):
            b.c.bump(tag + "ld", 16)
            if b.on(ldeng):
                if c == 0:
                    if tch == 0:
                        b.E.wait_ge(b.sem(dsem_name), b.c.get(dsem_name))
                    else:
                        b.E.wait_ge(b.sem(tag + "n"), b.c.get(tag + "n"))
                b.E.dma_start(out=blkout[:, c * 512:(c + 1) * 512],
                              in_=blk_dram.ap()[c * 128:(c + 1) * 128,
                                                tch * 512:(tch + 1) * 512]).then_inc(
                    b.sem(tag + "ld"), 16)
        emit_ln_tch(b, tag, tch,
                    lambda c: (blkout[:, c * 512:(c + 1) * 512], False),
                    "post", guard=(b.sem(tag + "ld"), b.c.get(tag + "ld")))
    b.fence(tag + "end")



MOE_ES = list(range(9))
MOE_STAGE = 4


def emit_moe(b):
    psA, psB = b.t["psA"], b.t["psB"]
    wup = b.t["l1_wup"].ap()
    wdn = b.t["l1_wdn"].ap()
    acc = b.t["acc_dram"]
    comb_dram = b.t["comb_dram"]
    rw_sb = b.carve(88 * 1024, 128, 64, BF16)
    combT = b.carve(84 * 1024, 8, T, BF16)
    crow = b.carve(88 * 1024 + 256, 1, 512, BF16)
    exps = b.t["moe_exps"]
    m1 = b.t["moe_m1"]
    m2 = b.t["moe_m2"]
    su = b.t["moe_su"]
    rb_bf = b.t["rb_bf"]

    b.c.bump("mew", 32)
    if b.on("gpsimd"):
        b.E.dma_start(out=rw_sb, in_=b.t["l1_rwT"].ap()[:, :]).then_inc(b.sem("mew"), 16)
        b.E.dma_start(out=rb_bf[0:1, :], in_=b.t["l1_rb"].ap()[0:1, :]).then_inc(
            b.sem("mew"), 16)
    for tt in range(16):
        nm = b.c.bump("merm")
        if b.on("tensor"):
            if tt == 0:
                b.E.wait_ge(b.sem("mew"), 32)
                b.E.wait_ge(b.sem("f_ln6end"), 3)
            if nm > 2:
                b.E.wait_ge(b.sem("mert"), nm - 2)
            pslot = psA[:, (tt % 2) * 512:(tt % 2) * 512 + 8]
            for dc in range(NDC):
                b.E.matmul(pslot, xln_ap(b, dc, slice(tt * 128, (tt + 1) * 128)),
                           rw_sb[:, dc * 8:(dc + 1) * 8],
                           start=(dc == 0), stop=False)
            b.E.matmul(pslot, b.t["onesrow_b"][0:1, :], rb_bf[0:1, :],
                       start=False, stop=True).then_inc(b.sem("merm"), 1)
        nv0 = b.c.bump("merv")
        if b.on("vector"):
            b.E.wait_ge(b.sem("merm"), nm)
            b.E.reduce_max(m1[:, 0:1], psA[:, (tt % 2) * 512:(tt % 2) * 512 + 8],
                           axis=AX.X, negate=True).then_inc(b.sem("merv"), 1)
        ns = b.c.bump("mers")
        if b.on("scalar"):
            b.E.wait_ge(b.sem("merv"), nv0)
            b.E.activation(out=exps[:, :], in_=psA[:, (tt % 2) * 512:(tt % 2) * 512 + 8],
                           func=AF.Exp, bias=m1[:, 0:1]).then_inc(b.sem("mers"), 1)
        steps = [b.c.bump("merv") for _ in range(9)]
        if b.on("vector"):
            b.E.wait_ge(b.sem("mers"), ns)
            b.E.reduce_max(m1[:, 0:1], exps[:, :], axis=AX.X).then_inc(b.sem("merv"), 1)
            b.E.wait_ge(b.sem("merv"), steps[0])
            b.E.tensor_scalar(b.t["moe_eq"][:, :], exps[:, :], m1[:, 0:1], 0.0,
                              OP.is_equal, OP.add).then_inc(b.sem("merv"), 1)
            b.E.wait_ge(b.sem("merv"), steps[1])
            b.E.scalar_tensor_tensor(b.t["moe_tmp"][:, :], b.t["moe_eq"][:, :],
                                     NEG, exps[:, :], OP.mult,
                                     OP.add).then_inc(b.sem("merv"), 1)
            b.E.wait_ge(b.sem("merv"), steps[2])
            b.E.reduce_max(m2[:, 0:1], b.t["moe_tmp"][:, :],
                           axis=AX.X).then_inc(b.sem("merv"), 1)
            b.E.wait_ge(b.sem("merv"), steps[3])
            b.E.tensor_scalar(b.t["moe_eq"][:, :], exps[:, :], m2[:, 0:1], 0.0,
                              OP.is_ge, OP.add).then_inc(b.sem("merv"), 1)
            b.E.tensor_add(su[:, 0:1], m1[:, 0:1], m2[:, 0:1]).then_inc(b.sem("merv"), 1)
            b.E.wait_ge(b.sem("merv"), steps[5])
            b.E.reciprocal(su[:, 1:2], su[:, 0:1]).then_inc(b.sem("merv"), 1)
            b.E.wait_ge(b.sem("merv"), steps[4])
            b.E.tensor_tensor(b.t["moe_tmp"][:, :], exps[:, :], b.t["moe_eq"][:, :],
                              OP.mult).then_inc(b.sem("merv"), 1)
            b.E.wait_ge(b.sem("merv"), steps[7])
            b.E.tensor_scalar_mul(b.t["moe_cmb"][:, :], b.t["moe_tmp"][:, :],
                                  su[:, 1:2]).then_inc(b.sem("merv"), 1)
        if tt == 15:
            b.dump("mecl", b.t["moe_cmb"][:, :], b.sem("merv"), steps[8])
            b.dump("mcex", exps[:, :], b.sem("merv"), steps[8])
        nvr = steps[8]
        ntt = b.c.bump("mert")
        if b.on("tensor"):
            b.E.wait_ge(b.sem("merv"), nvr)
            b.E.transpose(psB[0:8, 1536:1600].bitcast(BF16)[0:8, 0:128],
                          b.t["moe_cmb"][:, :], b.t["iden_bf"][:, :]).then_inc(
                b.sem("mert"), 1)
        b.c.bump("merv")
        if b.on("vector"):
            b.E.wait_ge(b.sem("mert"), ntt)
            b.E.tensor_copy(combT[0:8, tt * 128:(tt + 1) * 128],
                            psB[0:8, 1536:1600].bitcast(BF16)[0:8, 0:128]).then_inc(
                b.sem("merv"), 1)
    b.c.bump("merd", 16)
    if b.on("sync"):
        b.E.wait_ge(b.sem("merv"), b.c.get("merv"))
        b.E.dma_start(out=comb_dram.ap()[:, :], in_=combT[0:8, :]).then_inc(
            b.sem("merd"), 16)

    if "mecmb" in b.dbg:
        b.c.bump("dbg", 16)
        if b.on("sync"):
            b.E.wait_ge(b.sem("merd"), 16)
            b.E.dma_start(out=b.t["dbg_mecmb"].ap()[0:8, 0:T],
                          in_=comb_dram.ap()[:, :]).then_inc(b.sem("dbg"), 16)

    h_sb = b.carve(64 * 1024, 128, 16 * 512, BF16)
    ysb = b.carve(80 * 1024, 128, 512, F32)
    cbc = b.carve(82 * 1024, 128, 512, F32)

    def wup_sb(dc):
        return b.carve(dc * 4096, 128, 2048, BF16)

    def wdn_sb(kc):
        return b.carve(AOF["wsA"] + kc * 2048, 128, 1024, BF16)

    for e in MOE_ES:
        for dc in range(NDC):
            for fs in range(4):
                b.c.bump("meuw", 16)
                if b.on("gpsimd"):
                    if dc == 0 and fs == 0:
                        if e == MOE_ES[0]:
                            b.E.wait_ge(b.sem("f_ln6end"), 3)
                        else:
                            b.E.wait_ge(b.sem("mehm"), MOE_ES.index(e) * 16 * NTC)
                    b.E.dma_start(out=wup_sb(dc)[:, fs * 512:(fs + 1) * 512],
                                  in_=wup[e, dc, fs]).then_inc(b.sem("meuw"), 16)
        for kc in range(16):
            for j in range(2):
                b.c.bump("medw", 16)
                if b.on("gpsimd"):
                    if kc == 0 and j == 0 and e != MOE_ES[0]:
                        b.E.wait_ge(b.sem("meym"), MOE_ES.index(e) * 8 * NTC)
                    b.E.dma_start(out=wdn_sb(kc)[:, j * 512:(j + 1) * 512],
                                  in_=wdn[e, kc, j]).then_inc(b.sem("medw"), 16)
        nuw = b.c.get("meuw")
        ndw = b.c.get("medw")
        for tch in range(NTC):
            if MOE_STAGE < 2:
                continue
            grp = psA if tch % 2 == 0 else psB
            sl = slice(tch * 512, (tch + 1) * 512)
            for ft in range(16):
                nm = b.c.bump("mehm")
                if b.on("tensor"):
                    if ft == 0 and tch == 0:
                        b.E.wait_ge(b.sem("meuw"), nuw)
                        if e == MOE_ES[0]:
                            b.E.wait_ge(b.sem("mert"), b.c.get("mert"))
                            b.E.wait_ge(b.sem("merv"), b.c.get("merv"))
                    if nm > 2:
                        b.E.wait_ge(b.sem("mehe"), nm - 2)
                    if ft == 0 and b.c.get("meye"):
                        b.E.wait_ge(b.sem("meye"), b.c.get("meye"))
                    pslot = grp[:, (nm % 2) * 512:(nm % 2) * 512 + 512]
                    ins = None
                    for dc in range(NDC):
                        ins = b.E.matmul(pslot,
                                         wup_sb(dc)[:, ft * 128:(ft + 1) * 128],
                                         xln_ap(b, dc, sl),
                                         start=(dc == 0), stop=(dc == NDC - 1))
                    ins.then_inc(b.sem("mehm"), 1)
                ne = b.c.bump("mehe")
                if b.on("scalar"):
                    b.E.wait_ge(b.sem("mehm"), nm)
                    pslot = grp[:, (nm % 2) * 512:(nm % 2) * 512 + 512]
                    b.E.activation(out=h_sb[:, ft * 512:(ft + 1) * 512], in_=pslot,
                                   func=AF.Silu).then_inc(b.sem("mehe"), 1)
            if e == MOE_ES[0] and tch == 0:
                b.dump("meh0", h_sb, b.sem("mehe"), b.c.get("mehe"))
            if e > 0:
                b.c.bump("mecd", 16)
                if b.on("sync"):
                    b.E.wait_ge(b.sem("merd"), 16)
                    b.E.wait_ge(b.sem("mecv"), b.c.get("mecv"))
                    b.E.dma_start(out=crow,
                                  in_=comb_dram.ap()[e - 1:e, tch * 512:(tch + 1) * 512]).then_inc(
                        b.sem("mecd"), 16)
                ncb = b.c.bump("mecb")
                if b.on("tensor"):
                    b.E.wait_ge(b.sem("mecd"), b.c.get("mecd"))
                    if b.c.get("mecv") >= 1:
                        b.E.wait_ge(b.sem("mecv"), b.c.get("mecv"))
                    b.E.matmul(grp[:, 1024:1536], b.t["onesrow_b"][0:1, :], crow,
                               start=True, stop=True).then_inc(b.sem("mecb"), 1)
                b.c.bump("mecv")
                if b.on("scalar"):
                    b.E.wait_ge(b.sem("mecb"), ncb)
                    b.E.activation(out=cbc, in_=grp[:, 1024:1536],
                                   func=AF.Identity).then_inc(b.sem("mecv"), 1)
            for dt in range(8 if MOE_STAGE >= 3 else 0):
                nm = b.c.bump("meym")
                if b.on("tensor"):
                    if dt == 0 and tch == 0:
                        b.E.wait_ge(b.sem("medw"), ndw)
                    if dt == 0:
                        b.E.wait_ge(b.sem("mehe"), b.c.get("mehe"))
                    if nm > 2:
                        b.E.wait_ge(b.sem("meye"), nm - 2)
                    pslot = grp[:, (nm % 2) * 512:(nm % 2) * 512 + 512]
                    ins = None
                    for kc in range(16):
                        ins = b.E.matmul(pslot, wdn_sb(kc)[:, dt * 128:(dt + 1) * 128],
                                         h_sb[:, kc * 512:(kc + 1) * 512],
                                         start=(kc == 0), stop=(kc == 15))
                    ins.then_inc(b.sem("meym"), 1)
                ne = b.c.bump("meye")
                if b.on("scalar"):
                    b.E.wait_ge(b.sem("meym"), nm)
                    if ne > 1:
                        b.E.wait_ge(b.sem("mead"), 16 * (ne - 1))
                    pslot = grp[:, (nm % 2) * 512:(nm % 2) * 512 + 512]
                    b.E.activation(out=ysb, in_=pslot, func=AF.Identity).then_inc(
                        b.sem("meye"), 1)
                ng = b.c.bump("meg") if e > 0 else b.c.get("meg")
                if e > 0 and b.on("gpsimd"):
                    b.E.wait_ge(b.sem("meye"), ne)
                    b.E.wait_ge(b.sem("mecv"), b.c.get("mecv"))
                    b.E.tensor_tensor(ysb, ysb, cbc, OP.mult).then_inc(b.sem("meg"), 1)
                if MOE_STAGE < 4:
                    continue
                b.c.bump("mead", 16)
                if b.on("gpsimd"):
                    if e > 0:
                        b.E.wait_ge(b.sem("meg"), ng)
                    else:
                        b.E.wait_ge(b.sem("meye"), ne)
                    if b.c.get("mead") > 16:
                        b.E.wait_ge(b.sem("mead"), b.c.get("mead") - 16)
                    b.E.dma_start(out=acc.ap()[dt * 128:(dt + 1) * 128, sl],
                                  in_=ysb,
                                  accum_op=(OP.bypass if e == MOE_ES[0] else OP.add)).then_inc(
                        b.sem("mead"), 16)

    if MOE_STAGE >= 4:
        emit_postln_from_dram(b, "ln7", acc, "mead")
    else:
        b.fence("ln7end")



def build_program(parts, dbg=(), salt=0):
    nc = bass.Bass("TRN2", target_bir_lowering=False)
    es = ExitStack()
    b = B(nc, es)
    b.dbg = list(dbg)
    l0a = "l0attn" in parts
    b.has_l0a = l0a
    l0f = "l0ffn" in parts
    l1a = "l1attn" in parts
    l1f = "l1ffn" in parts

    b.t["x_in"] = nc.dram_tensor("x_in", [D, T], F32, kind="ExternalInput")
    b.t["x_out"] = nc.dram_tensor("x_out", [D, T], F32, kind="ExternalOutput")
    if salt:
        nc.dram_tensor("salt%d" % salt, [1, 128], F32, kind="ExternalInput")
    ext = [("c_iden", [128, 128]), ("c_irev", [128, 128])]
    if l0a:
        ext += [("l0_wqkv", [NDC, 6, 128, 512]), ("l0_wout", [NDC, 2, 128, 512]),
                ("c_cmask", [128, 128]), ("c_negf", [128, 128]),
                ("c_swap", [128, 128]), ("c_cos2", [128, T]), ("c_sin2", [128, T])]
    if l0f:
        ext += [("l0_wr", [128, NDC]), ("l0_w1", [NDC, 8, 128, 512]),
                ("l0_b1", [1, 4096]), ("l0_w2", [32, 2, 128, 512]), ("l0_b2", [1, D])]
    if l1a:
        ext += [("l1_wqkv", [NDC, 6, 128, 512]), ("l1_bqkv", [1, 3 * D]),
                ("l1_wd", [NDC, 2, 128, 512]), ("l1_bd", [1, D]),
                ("l1_tbl", [16, 32]), ("c_oht", [32, 4096])]
    if l1f:
        ext += [("l1_rwT", [128, NDC * 8]), ("l1_rb", [1, 8]),
                ("l1_wup", [9, NDC, 4, 128, 512]), ("l1_wdn", [9, 16, 2, 128, 512])]
    for nm, shp in ext:
        b.t[nm] = nc.dram_tensor(nm, shp, F32, kind="ExternalInput")
    if l1a:
        b.t["exp_dram"] = nc.dram_tensor("exp_dram", [16, 4352], BF16)
    if l0f:
        b.t["scores_dram"] = nc.dram_tensor("scores_dram", [T], F32)
        b.t["wf_dram"] = nc.dram_tensor("wf_dram", [T], F32)
        b.t["h_dram"] = nc.dram_tensor("h_dram", [4096, T], BF16)
        b.t["blk_dram"] = nc.dram_tensor("blk_dram", [D, T], BF16)
    if l1f:
        b.t["comb_dram"] = nc.dram_tensor("comb_dram", [8, T], BF16)
        b.t["acc_dram"] = nc.dram_tensor("acc_dram", [D, T], F32)
    for d in b.dbg:
        b.t["dbg_" + d] = nc.dram_tensor("dbg_" + d, [128, 8 * T], BF16,
                                         kind="ExternalOutput")

    b.t["x"] = es.enter_context(nc.sbuf_tensor("x", [128, NDC * T], F32))
    b.t["xln"] = es.enter_context(nc.sbuf_tensor("xln", [128, NDC * T], BF16))
    b.t["scratch"] = es.enter_context(
        nc.sbuf_tensor("scratch", [128, SCRATCH_BYTES // 2], BF16))
    smalls = [("iden_bf", [128, 128], BF16), ("irev_bf", [128, 128], BF16),
              ("cmask_bf", [128, 128], BF16), ("negf_bf", [128, 128], BF16),
              ("swap_bf", [128, 128], BF16), ("diag", [128, 256], BF16),
              ("ones_b", [128, 1], BF16), ("onescol_f", [128, 1], F32),
              ("onesrow_f", [1, 128], F32), ("nonesrow_f", [1, 128], F32),
              ("onesrow_b", [1, 128], BF16),
              ("negmax", [128, 2], F32), ("sums", [128, 2], F32),
              ("recip", [128, 2], F32), ("iden16", [16, 16], F32),
              ("bq8_col", [128, 8], F32), ("bk_col", [128, 8], F32),
              ("bv_col", [128, 8], F32), ("bvb_col", [128, 8], BF16),
              ("bd_col", [128, 8], F32), ("pb_col", [128, 8], F32),
              ("b1_col", [128, 32], F32), ("b2_col", [128, 8], F32),
              ("sw11", [1, 1], F32), ("scores_p", [128, 16], F32),
              ("ranks", [128, 16], F32), ("rk1", [128, 1], F32),
              ("rksc", [128, 64], F32), ("sel16", [128, 16], F32),
              ("rk116", [128, 16], F32),
              ("wfull_p", [128, 16], F32), ("rb_bf", [1, 8], BF16),
              ("moe_exps", [128, 8], F32), ("moe_m1", [128, 1], F32),
              ("moe_m2", [128, 1], F32), ("moe_su", [128, 2], F32),
              ("moe_eq", [128, 8], F32), ("moe_tmp", [128, 8], F32),
              ("moe_cmb", [128, 8], BF16), ("epsc", [128, 1], F32),
              ("fdv", [1, 1], F32), ("fds", [1, 1], F32), ("fdg", [1, 1], F32)]
    for nm, shp, dt in smalls:
        b.t[nm] = es.enter_context(nc.sbuf_tensor(nm, shp, dt))
    nc.const_aps.register(F32, EPS, b.t["epsc"][:, 0:1])
    b.t["psA"] = es.enter_context(nc.psum_tensor("psA", [128, 2048], F32))
    b.t["psB"] = es.enter_context(nc.psum_tensor("psB", [128, 2048], F32))

    if l1f:
        final = "ln7end"
    elif l1a:
        final = "a1done"
    elif l0f:
        final = "ln3end"
    else:
        final = "a0done"

    def pass_for(eng):
        def f(E):
            b.eng = eng
            b.E = E
            b.c = Ctr()
            emit_load_x(b)
            emit_consts(b, l0a)
            emit_bias_cols(b, l0f, l1a)
            if l0a:
                emit_pre_ln(b, "ln0")
                b.dump("xln0", b.t["xln"][:, :], b.sem("ln0n"), b.c.get("ln0n"))
                emit_attention(b, 0)
            if l0f:
                emit_mod(b)
            if l1a:
                emit_relpos(b)
                emit_pre_ln(b, "ln4")
                emit_attention(b, 1)
            if l1f:
                emit_pre_ln(b, "ln6")
                emit_moe(b)
            emit_store_x(b, final)
        return f

    with nc.Block() as block:
        block.sync(pass_for("sync"))
        block.tensor(pass_for("tensor"))
        block.vector(pass_for("vector"))
        block.scalar(pass_for("scalar"))
        block.gpsimd(pass_for("gpsimd"))
    es.close()
    return nc


_PROG_CACHE = {}


def _get_prog(key, salt=0):
    ck = (key, salt)
    if ck not in _PROG_CACHE:
        _PROG_CACHE[ck] = build_program(key.split("+"), salt=salt)
    return _PROG_CACHE[ck]


STAGES = ["l0attn", "l0ffn", "l1attn", "l1ffn"]
_GOOD_SALT = {}


def _launch(sk, in_maps, ncores):
    import sys
    salts = [_GOOD_SALT.get(sk, 0)] + [s for s in range(8) if s != _GOOD_SALT.get(sk, 0)]
    last = None
    for salt in salts:
        nc = _get_prog(sk, salt)
        ims = in_maps
        if salt:
            ims = [dict(m, **{"salt%d" % salt: np.zeros((1, 128), np.float32)})
                   for m in in_maps]
        try:
            res = run_bass_kernel_spmd(nc, ims, core_ids=list(range(ncores)))
            out = [np.ascontiguousarray(np.asarray(r["x_out"])) for r in res.results]
            _GOOD_SALT[sk] = salt
            return out
        except Exception as e:
            last = e
            print("[kernel] stage %s salt %d raised %r; retrying" % (sk, salt, e),
                  flush=True, file=sys.stderr)
    raise last


def kernel(**inputs):
    g = make_weight_map(inputs)
    x = np.asarray(inputs["L0_attn_x"], np.float32)
    ncores = 8
    xs = [np.ascontiguousarray(x[c].T) for c in range(ncores)]
    for sk in STAGES:
        import sys
        print("[kernel] stage", sk, flush=True, file=sys.stderr)
        in_maps = []
        for c in range(ncores):
            m = dict(g)
            m["x_in"] = xs[c]
            in_maps.append(m)
        xs = _launch(sk, in_maps, ncores)
    return np.stack([xv.T for xv in xs]).astype(np.float32)
